# version 29
# baseline (speedup 1.0000x reference)
import numpy as np
import concourse.bass as bass
import concourse.mybir as mybir
from concourse.bass_utils import run_bass_kernel_spmd
from contextlib import ExitStack

F32 = mybir.dt.float32
BF16 = mybir.dt.bfloat16
I8 = mybir.dt.int8
AX = mybir.AxisListType.X
AF = mybir.ActivationFunctionType
OP = mybir.AluOpType

NCORE = 8
NSEQ = 2
T = 2048
CH = 128
NCH = T // CH
D = 1024
NCC = D // 128
NPAIR = 8
EPS_LN = 1e-5
EPS_ATTN = 1e-6
INV_D = 1.0 / 1024.0
NXY = 6


class Ev:
    __slots__ = ("sem", "val", "eng", "is_async")

    def __init__(self, sem, val, eng, is_async):
        self.sem, self.val, self.eng, self.is_async = sem, val, eng, is_async


class Res:
    __slots__ = ("w", "r")

    def __init__(self):
        self.w = None
        self.r = []


class Prog:
    def __init__(self, sems):
        self.sems = sems
        self.count = {k: 0 for k in sems}
        self.steps = {e: [] for e in ("sync", "gpsimd", "vector", "scalar", "tensor")}

    GP_COMPUTE_TO_VECTOR = False

    def emit(self, eng, fn, reads=(), writes=(), deps=(), sig=None, dma=False):
        if Prog.GP_COMPUTE_TO_VECTOR and eng == "gpsimd" and not dma:
            eng = "vector"
        evs = list(deps)
        for r in reads:
            if r.w is not None:
                evs.append(r.w)
        for w in writes:
            if w.w is not None:
                evs.append(w.w)
            evs.extend(w.r)
        need = {}
        for ev in evs:
            if ev is None:
                continue
            need[ev.sem] = max(need.get(ev.sem, 0), ev.val)
        ev_out = None
        if sig is not None:
            amt = 16 if dma else 1
            self.count[sig] += amt
            ev_out = Ev(sig, self.count[sig], eng, dma)
        self.steps[eng].append((need, fn, sig, 16 if dma else 1))
        for r in reads:
            if ev_out is not None:
                r.r.append(ev_out)
        for w in writes:
            w.w = ev_out
            w.r = []
        return ev_out

    def section(self, eng_name, eng):
        satisfied = {}
        for need, fn, sig, amt in self.steps[eng_name]:
            for k in sorted(need):
                v = need[k]
                if satisfied.get(k, 0) < v:
                    eng.wait_ge(self.sems[k], v)
                    satisfied[k] = v
            inst = fn(eng)
            if sig is not None:
                inst.then_inc(self.sems[sig], amt)


def build_nc(debug=False):
    nc = bass.Bass("TRN2", target_bir_lowering=False)
    x_h = nc.dram_tensor("x", [NSEQ * T, D], F32, kind="ExternalInput").ap()
    wqkvT_h = nc.dram_tensor("wqkvT", [D, 3 * D], F32, kind="ExternalInput").ap()
    wprojT_h = nc.dram_tensor("wprojT", [D, D], F32, kind="ExternalInput").ap()
    wffnT_h = nc.dram_tensor("wffnT", [D, D], I8, kind="ExternalInput").ap()
    wscale_h = nc.dram_tensor("wscale", [1, D], F32, kind="ExternalInput").ap()
    fbias_h = nc.dram_tensor("fbias", [1, D], F32, kind="ExternalInput").ap()
    bproj_h = nc.dram_tensor("bproj", [1, D], F32, kind="ExternalInput").ap()
    out_h = nc.dram_tensor("out", [NSEQ * T, D], F32, kind="ExternalOutput").ap()
    y1sp_h = nc.dram_tensor("y1sp", [NSEQ * T, D], F32,
                            kind="ExternalOutput" if debug else "Internal").ap()
    if debug:
        y2d_h = nc.dram_tensor("y2d", [NSEQ * T, D], F32, kind="ExternalOutput").ap()
        kvd_h = nc.dram_tensor("kvd", [128, NPAIR * 130], F32, kind="ExternalOutput").ap()
        qfmd_h = nc.dram_tensor("qfmd", [128, NCH, NPAIR, 128], BF16, kind="ExternalOutput").ap()
        rhsBd_h = nc.dram_tensor("rhsBd", [128, NPAIR, 130], BF16, kind="ExternalOutput").ap()
        atd_h = nc.dram_tensor("atd", [128, NCH, D], BF16, kind="ExternalOutput").ap()

    with ExitStack() as st:
        def sb(name, shape, dt):
            return st.enter_context(nc.sbuf_tensor(name, shape, dt))

        def psb(name, shape, dt):
            return st.enter_context(nc.psum_tensor(name, shape, dt))

        Wq = sb("Wq", [128, NCC, 3 * D], BF16)
        Wp = sb("Wp", [128, NCC, D], BF16)
        Wf = sb("Wf", [128, NCC, D], BF16)
        Wfi8 = sb("Wfi8", [128, 2, D], I8)
        wsrow = sb("wsrow", [128, D], F32)
        hbrow = sb("hbrow", [128, D], BF16)
        bprow = sb("bprow", [128, D], BF16)
        rowst = sb("rowst", [1, D], F32)
        ones1 = sb("ones1", [1, 128], F32)
        ones1b = sb("ones1b", [1, 128], BF16)
        xy = [sb(f"xy{j}", [128, D], F32) for j in range(NXY)]
        qpre = sb("qpre", [128, D], F32)
        sgn = sb("sgn", [128, D], F32)
        rlu = sb("rlu", [128, D], F32)
        xq_i8 = sb("xq_i8", [128, D], I8)
        xq_tm = sb("xq_tm", [128, D], BF16)
        xq_fm = [sb(f"xq_fm{j}", [128, NCC, 128], BF16) for j in range(3)]
        ln_tm = sb("ln_tm", [128, D], BF16)
        ln_fm = [sb(f"ln_fm{j}", [128, NCC, 128], BF16) for j in range(3)]
        q_tm = sb("q_tm", [128, D], BF16)
        qfm = sb("qfm", [128, NCH, NPAIR, 128], BF16)
        k_tm = sb("k_tm", [128, D], BF16)
        vpad = [sb(f"vpad{j}", [128, 16 * 65], BF16) for j in range(2)]
        kvacc = sb("kvacc", [128, NPAIR * 130], F32)
        rhsB = sb("rhsB", [128, NPAIR, 130], BF16)
        at_tm = sb("at_tm", [128, D], BF16)
        at_fm = [sb(f"at_fm{j}", [128, NCC, 128], BF16) for j in range(3)]
        at_ps = sb("at_ps", [128, NPAIR * 130], F32)
        rnorm = sb("rnorm", [128, 16], F32)
        rnrec = sb("rnrec", [128, 16], F32)

        def s2(name):
            return [sb(name + str(j), [128, 1], F32) for j in range(2)]
        msum = s2("msum")
        mmean = s2("mmean")
        negm = s2("negm")
        vsum = s2("vsum")
        sstd = s2("sstd")
        rstd = s2("rstd")
        mx = s2("mx")
        mn = s2("mn")
        tt1 = s2("tt1")
        tt2 = s2("tt2")
        amax = s2("amax")
        xsc = s2("xsc")
        rxs = s2("rxs")
        gq = s2("gq")
        xs05 = s2("xs05")
        msum2 = s2("msum2")
        epsln = sb("epsln", [128, 1], F32)
        epsat = sb("epsat", [128, 1], F32)
        dt_f = sb("dt_f", [128, 1], F32)
        dt_i8 = sb("dt_i8", [128, 1], I8)
        dt_f2 = sb("dt_f2", [128, 1], F32)
        hdet = sb("hdet", [128, 1], F32)

        pbA = psb("pbA", [128, 1024], F32)
        pbB = psb("pbB", [128, 1024], F32)
        pbC = psb("pbC", [128, 1024], F32)
        pbD = psb("pbD", [128, 1024], F32)

        sems = {}
        for name in ("s_x0", "s_x1", "s_x2", "s_x3", "s_x4", "s_x5", "s_row", "s_wf", "s_wq", "s_wp",
                     "s_tpxq", "s_tpln", "s_tpat", "s_tpq",
                     "s_y1w0", "s_y1w1", "s_y1w2", "s_y1w3", "s_y1w4", "s_y1w5",
                     "s_out0", "s_out1", "s_out2", "s_out3", "s_out4", "s_out5",
                     "s_dbg0", "s_dbg1", "s_dbg2", "s_dbg3", "s_dbg4", "s_dbg5",
                     "s_dbgk", "s_dbgq", "s_dbgr", "s_dbga",
                     "s_t", "s_v", "s_s", "s_g"):
            sems[name] = st.enter_context(nc.semaphore(name))
        block = st.enter_context(nc.Block())

        P = Prog(sems)

        R_xy = [Res() for _ in range(NXY)]
        R_Wq = [Res() for _ in range(NCC)]
        R_Wp = [Res() for _ in range(NCC)]
        R_Wf, R_Wfi8 = Res(), Res()
        R_wsrow, R_hbrow, R_bprow, R_rowst, R_ones = Res(), Res(), Res(), Res(), Res()
        R_onesb = Res()
        R_qpre, R_sgn, R_rlu = Res(), Res(), Res()
        R_xqi8, R_xqtm = Res(), Res()
        R_xqfm = [Res(), Res(), Res()]
        R_lntm = Res()
        R_lnfm = [Res(), Res(), Res()]
        R_qtm = Res()
        R_qfm = [Res() for _ in range(NCH)]
        R_ktm = Res()
        R_vpad = [Res(), Res()]
        R_kvacc = Res()
        R_rhsB = Res()
        R_attm = Res()
        R_atfm = [Res(), Res(), Res()]
        R_atps = Res()
        R_rnorm, R_rnrec = Res(), Res()
        stat_res = {}
        for t in (msum, mmean, negm, vsum, sstd, rstd, mx, mn, tt1, tt2, amax, xsc, rxs, gq, xs05, msum2):
            stat_res[id(t)] = [Res(), Res()]

        def RS(t, d):
            return stat_res[id(t)][d]
        R_eps, R_epsat, R_det = Res(), Res(), Res()
        R_pA = [Res(), Res()]
        R_pB = [Res(), Res()]
        R_pC = [Res(), Res()]
        R_pD = [Res(), Res()]
        R_areg = [Res() for _ in range(NPAIR)]
        R_kvreg = [Res() for _ in range(NPAIR)]
        R_y1ch = [[Res() for _ in range(NCH)] for _ in range(NSEQ)]

        vpad3 = [vpad[j][:, :].rearrange("p (h m) -> p h m", m=65) for j in range(2)]

        P.emit("vector", lambda e: e.memset(ones1[:, :], 1.0), writes=[R_ones], sig="s_v")
        P.emit("vector", lambda e: e.memset(ones1b[:, :], 1.0), writes=[R_onesb], sig="s_v")
        P.emit("vector", lambda e: e.memset(epsln[:, :], EPS_LN), writes=[R_eps], sig="s_v")
        P.emit("vector", lambda e: e.memset(epsat[:, :], EPS_ATTN), writes=[R_epsat], sig="s_v")
        P.emit("vector", lambda e: e.memset(dt_f[:, :], 0.7), writes=[R_det], sig="s_v")
        P.emit("vector", lambda e: e.tensor_copy(dt_i8[:, :], dt_f[:, :]), reads=[R_det], writes=[R_det], sig="s_v")
        P.emit("vector", lambda e: e.tensor_copy(dt_f2[:, :], dt_i8[:, :]), reads=[R_det], writes=[R_det], sig="s_v")
        P.emit("vector", lambda e: e.tensor_scalar(hdet[:, :], dt_f2[:, :], -1.0, -0.5, OP.add, OP.mult),
               reads=[R_det], writes=[R_det], sig="s_v")
        for j in range(2):
            P.emit("vector", lambda e, j=j: e.memset(vpad3[j][:, :, 64:65], 1.0),
                   writes=[R_vpad[j]], sig="s_v")
        P.emit("vector", lambda e: e.memset(rhsB[:, :, :], 0.0), writes=[R_rhsB], sig="s_v")

        for src, dst_row, scale, rres in ((wscale_h, wsrow, None, R_wsrow),
                                          (fbias_h, hbrow, 0.5, R_hbrow),
                                          (bproj_h, bprow, None, R_bprow)):
            P.emit("sync", lambda e, src=src: e.dma_start(out=rowst[:, :], in_=src),
                   writes=[R_rowst], sig="s_row", dma=True)
            for half in range(2):
                P.emit("tensor", lambda e, half=half: e.matmul(
                    pbA[:, half * 512:(half + 1) * 512], lhsT=ones1[:, :],
                    rhs=rowst[:, half * 512:(half + 1) * 512], start=True, stop=True),
                    reads=[R_rowst, R_ones], writes=[R_pA[half]], sig="s_t")
            if scale is None:
                P.emit("vector", lambda e, dst_row=dst_row: e.tensor_copy(dst_row[:, :], pbA[:, :]),
                       reads=[R_pA[0], R_pA[1]], writes=[rres], sig="s_v")
            else:
                P.emit("vector", lambda e, dst_row=dst_row, scale=scale:
                       e.tensor_scalar(dst_row[:, :], pbA[:, :], scale, None, OP.mult),
                       reads=[R_pA[0], R_pA[1]], writes=[rres], sig="s_v")

        for r in range(4):
            src = wffnT_h[r * 256:(r + 1) * 256, :].rearrange("(cc p) o -> p cc o", p=128)
            P.emit("sync", lambda e, src=src: e.dma_start(out=Wfi8[:, :, :], in_=src),
                   writes=[R_Wfi8], sig="s_wf", dma=True)
            P.emit("vector", lambda e, r=r: e.tensor_copy(Wf[:, 2 * r:2 * r + 2, :], Wfi8[:, :, :]),
                   reads=[R_Wfi8], writes=[R_Wf], sig="s_v")
        for cc in range(NCC):
            src = wqkvT_h[cc * 128:(cc + 1) * 128, :]
            P.emit("gpsimd", lambda e, src=src, cc=cc: e.dma_start(out=Wq[:, cc, :], in_=src),
                   writes=[R_Wq[cc]], sig="s_wq", dma=True)
        for cc in range(NCC):
            src = wprojT_h[cc * 128:(cc + 1) * 128, :]
            P.emit("gpsimd", lambda e, src=src, cc=cc: e.dma_start(out=Wp[:, cc, :], in_=src),
                   writes=[R_Wp[cc]], sig="s_wp", dma=True)

        kv_first = {}
        y1_last_ev = {}

        def load_x(s, i):
            b = i % NXY
            src = x_h[s * T + i * CH: s * T + (i + 1) * CH, :]
            P.emit("sync", lambda e, src=src, b=b: e.dma_start(out=xy[b][:, :], in_=src),
                   writes=[R_xy[b]], sig=f"s_x{b}", dma=True)

        def load_y1(s, i):
            b = i % NXY
            src = y1sp_h[s * T + i * CH: s * T + (i + 1) * CH, :]
            P.emit("sync", lambda e, src=src, b=b: e.dma_start(out=xy[b][:, :], in_=src),
                   reads=[R_y1ch[s][i]], deps=list(y1_last_ev.get(s, {}).values()),
                   writes=[R_xy[b]], sig=f"s_x{b}", dma=True)

        def quant_chain(d, b, have_mean, fm):
            if not have_mean:
                P.emit("vector", lambda e: e.tensor_reduce(msum[d][:, :], xy[b][:, :], AX, OP.add),
                       reads=[R_xy[b]], writes=[RS(msum, d)], sig="s_v")
                src_ms = msum
            else:
                src_ms = msum2
            P.emit("vector", lambda e: e.tensor_reduce(mx[d][:, :], xy[b][:, :], AX, OP.max),
                   reads=[R_xy[b]], writes=[RS(mx, d)], sig="s_v")
            P.emit("vector", lambda e: e.tensor_reduce(mn[d][:, :], xy[b][:, :], AX, OP.min),
                   reads=[R_xy[b]], writes=[RS(mn, d)], sig="s_v")
            P.emit("gpsimd", lambda e: e.tensor_scalar(mmean[d][:, :], src_ms[d][:, :], INV_D, None, OP.mult),
                   reads=[RS(src_ms, d)], writes=[RS(mmean, d)], sig="s_g")
            P.emit("gpsimd", lambda e: e.tensor_scalar(negm[d][:, :], src_ms[d][:, :], -INV_D, None, OP.mult),
                   reads=[RS(src_ms, d)], writes=[RS(negm, d)], sig="s_g")
            P.emit("scalar", lambda e: e.activation(qpre[:, :], xy[b][:, :], AF.Square,
                                                    bias=negm[d][:, :], accum_out=vsum[d][:, :]),
                   reads=[R_xy[b], RS(negm, d)], writes=[RS(vsum, d), R_qpre], sig="s_s")
            P.emit("scalar", lambda e: e.activation(sstd[d][:, :], vsum[d][:, :], AF.Sqrt,
                                                    bias=epsln[:, :], scale=INV_D),
                   reads=[RS(vsum, d), R_eps], writes=[RS(sstd, d)], sig="s_s")
            P.emit("vector", lambda e: e.reciprocal(rstd[d][:, :], sstd[d][:, :]),
                   reads=[RS(sstd, d)], writes=[RS(rstd, d)], sig="s_v")
            P.emit("gpsimd", lambda e: e.tensor_scalar(tt1[d][:, :], mx[d][:, :], mmean[d][:, :], None, OP.subtract),
                   reads=[RS(mx, d), RS(mmean, d)], writes=[RS(tt1, d)], sig="s_g")
            P.emit("gpsimd", lambda e: e.tensor_scalar(tt2[d][:, :], mn[d][:, :], mmean[d][:, :], -1.0,
                                                       OP.subtract, OP.mult),
                   reads=[RS(mn, d), RS(mmean, d)], writes=[RS(tt2, d)], sig="s_g")
            P.emit("vector", lambda e: e.tensor_tensor(tt1[d][:, :], tt1[d][:, :], tt2[d][:, :], OP.max),
                   reads=[RS(tt1, d), RS(tt2, d)], writes=[RS(tt1, d)], sig="s_v")
            P.emit("gpsimd", lambda e: e.tensor_scalar(amax[d][:, :], tt1[d][:, :], rstd[d][:, :], None, OP.mult),
                   reads=[RS(tt1, d), RS(rstd, d)], writes=[RS(amax, d)], sig="s_g")
            P.emit("vector", lambda e: e.tensor_scalar(xsc[d][:, :], amax[d][:, :], EPS_ATTN, 1.0 / 127.0,
                                                       OP.max, OP.mult),
                   reads=[RS(amax, d)], writes=[RS(xsc, d)], sig="s_v")
            P.emit("vector", lambda e: e.reciprocal(rxs[d][:, :], xsc[d][:, :]),
                   reads=[RS(xsc, d)], writes=[RS(rxs, d)], sig="s_v")
            P.emit("gpsimd", lambda e: e.tensor_scalar(gq[d][:, :], rstd[d][:, :], rxs[d][:, :], None, OP.mult),
                   reads=[RS(rstd, d), RS(rxs, d)], writes=[RS(gq, d)], sig="s_g")
            P.emit("gpsimd", lambda e: e.tensor_scalar(xs05[d][:, :], xsc[d][:, :], 0.5, None, OP.mult),
                   reads=[RS(xsc, d)], writes=[RS(xs05, d)], sig="s_g")
            P.emit("vector", lambda e: e.tensor_scalar(qpre[:, :], xy[b][:, :], mmean[d][:, :], gq[d][:, :],
                                                       OP.subtract, OP.mult),
                   reads=[R_xy[b], RS(mmean, d), RS(gq, d)], writes=[R_qpre], sig="s_v")
            P.emit("scalar", lambda e: e.activation(sgn[:, :], qpre[:, :], AF.Sign),
                   reads=[R_qpre], writes=[R_sgn], sig="s_s")
            P.emit("vector", lambda e: e.scalar_tensor_tensor(xq_i8[:, :], sgn[:, :], hdet[:, :], qpre[:, :],
                                                              OP.mult, OP.add),
                   reads=[R_sgn, R_qpre, R_det], writes=[R_xqi8], sig="s_v")
            P.emit("vector", lambda e: e.tensor_copy(xq_tm[:, :], xq_i8[:, :]),
                   reads=[R_xqi8], writes=[R_xqtm], sig="s_v")
            P.emit("sync", lambda e, fm=fm: e.dma_start_transpose(xq_fm[fm][:, :, :], xq_tm[:, :]),
                   reads=[R_xqtm], writes=[R_xqfm[fm]], sig="s_tpxq", dma=True)

        def ffn_mms(d, pb, R_p):
            def fn(e):
                last = None
                for cc in range(NCC):
                    for ob in range(2):
                        last = e.matmul(pb[:, ob * 512:(ob + 1) * 512],
                                        lhsT=xq_fm[d][:, cc, :],
                                        rhs=Wf[:, cc, ob * 512:(ob + 1) * 512],
                                        start=(cc == 0), stop=(cc == NCC - 1))
                return last
            P.emit("tensor", fn, reads=[R_xqfm[d], R_Wf], writes=[R_p[0], R_p[1]], sig="s_t")

        def xb_pre(b, row, rrow):
            P.emit("gpsimd", lambda e, b=b, row=row: e.tensor_tensor(xy[b][:, :], xy[b][:, :], row[:, :], OP.add),
                   reads=[R_xy[b], rrow], writes=[R_xy[b]], sig="s_g")

        def ffn_evac(s, i, d, b, pb, R_p, accum_next_mean, store_y1):
            P.emit("vector", lambda e: e.scalar_tensor_tensor(qpre[:, :], pb[:, :], xs05[d][:, :],
                                                              wsrow[:, :], OP.mult, OP.mult),
                   reads=[R_p[0], R_p[1], RS(xs05, d), R_wsrow], writes=[R_qpre], sig="s_v")
            if accum_next_mean:
                P.emit("vector", lambda e: e.scalar_tensor_tensor(xy[b][:, :], qpre[:, :], 1.0, xy[b][:, :],
                                                                  OP.mult, OP.add, accum_out=msum2[d][:, :]),
                       reads=[R_qpre, R_xy[b]], writes=[R_xy[b], RS(msum2, d)], sig="s_v")
            else:
                P.emit("vector", lambda e: e.scalar_tensor_tensor(xy[b][:, :], qpre[:, :], 1.0, xy[b][:, :],
                                                                  OP.mult, OP.add),
                       reads=[R_qpre, R_xy[b]], writes=[R_xy[b]], sig="s_v")
            if store_y1:
                dst = y1sp_h[s * T + i * CH: s * T + (i + 1) * CH, :]
                ev = P.emit("gpsimd", lambda e, dst=dst, b=b: e.dma_start(out=dst, in_=xy[b][:, :]),
                            reads=[R_xy[b]], writes=[R_y1ch[s][i]], sig=f"s_y1w{b}", dma=True)
                y1_last_ev.setdefault(s, {})[b] = ev

        def ln2_chain(d, b, fm):
            P.emit("gpsimd", lambda e: e.tensor_scalar(mmean[d][:, :], msum2[d][:, :], INV_D, None, OP.mult),
                   reads=[RS(msum2, d)], writes=[RS(mmean, d)], sig="s_g")
            P.emit("gpsimd", lambda e: e.tensor_scalar(negm[d][:, :], msum2[d][:, :], -INV_D, None, OP.mult),
                   reads=[RS(msum2, d)], writes=[RS(negm, d)], sig="s_g")
            P.emit("scalar", lambda e: e.activation(qpre[:, :], xy[b][:, :], AF.Square,
                                                    bias=negm[d][:, :], accum_out=vsum[d][:, :]),
                   reads=[R_xy[b], RS(negm, d)], writes=[RS(vsum, d), R_qpre], sig="s_s")
            P.emit("scalar", lambda e: e.activation(sstd[d][:, :], vsum[d][:, :], AF.Sqrt,
                                                    bias=epsln[:, :], scale=INV_D),
                   reads=[RS(vsum, d), R_eps], writes=[RS(sstd, d)], sig="s_s")
            P.emit("vector", lambda e: e.reciprocal(rstd[d][:, :], sstd[d][:, :]),
                   reads=[RS(sstd, d)], writes=[RS(rstd, d)], sig="s_v")
            P.emit("vector", lambda e: e.tensor_scalar(ln_tm[:, :], xy[b][:, :], mmean[d][:, :], rstd[d][:, :],
                                                       OP.subtract, OP.mult),
                   reads=[R_xy[b], RS(mmean, d), RS(rstd, d)], writes=[R_lntm], sig="s_v")
            P.emit("scalar", lambda e, fm=fm: e.dma_start_transpose(ln_fm[fm][:, :, :], ln_tm[:, :]),
                   reads=[R_lntm], writes=[R_lnfm[fm]], sig="s_tpln", dma=True)

        def qkv_part_mms(d, pb, R_p, e_lo):
            def fn(e):
                last = None
                for cc in range(NCC):
                    for ob in range(2):
                        last = e.matmul(pb[:, ob * 512:(ob + 1) * 512],
                                        lhsT=ln_fm[d][:, cc, :],
                                        rhs=Wq[:, cc, e_lo + ob * 512: e_lo + (ob + 1) * 512],
                                        start=(cc == 0), stop=(cc == NCC - 1))
                return last
            P.emit("tensor", fn, reads=[R_lnfm[d]] + R_Wq, writes=[R_p[0], R_p[1]], sig="s_t")

        def phi_evac(pb, R_p, dst_ap, R_dst):
            P.emit("scalar", lambda e: e.activation(sgn[:, :], pb[:, :], AF.Relu, scale=-1.0),
                   reads=[R_p[0], R_p[1]], writes=[R_sgn], sig="s_s")
            P.emit("scalar", lambda e: e.activation(rlu[:, :], sgn[:, :], AF.Exp, scale=-1.0),
                   reads=[R_sgn], writes=[R_rlu], sig="s_s")
            P.emit("vector", lambda e: e.scalar_tensor_tensor(dst_ap, pb[:, :], 0.0, rlu[:, :],
                                                              OP.max, OP.add),
                   reads=[R_p[0], R_p[1], R_rlu], writes=[R_dst], sig="s_v")

        def kv_region(p):
            if p < 3:
                return pbD, 130 * p, R_pD[0]
            if p < 6:
                return pbD, 512 + 130 * (p - 3), R_pD[1]
            return pbC, 130 * (p - 6), R_pC[0]

        def emit_kv(s, i, d):
            first = kv_first.setdefault(s, i) == i
            for p in range(NPAIR):
                pb, off, rbank = kv_region(p)
                P.emit("tensor", lambda e, pb=pb, off=off, p=p, d=d: e.matmul(
                    pb[:, off:off + 130], lhsT=k_tm[:, p * 128:(p + 1) * 128],
                    rhs=vpad[d][:, p * 130:(p + 1) * 130], start=True, stop=True),
                    reads=[R_ktm, R_vpad[d]], writes=[R_kvreg[p], rbank], sig="s_t")
            groups = ((pbD, 0, 0, 390, (0, 1, 2), R_pD[0]),
                      (pbD, 512, 390, 390, (3, 4, 5), R_pD[1]),
                      (pbC, 0, 780, 260, (6, 7), R_pC[0]))
            for pb, poff, aoff, width, ps, rbank in groups:
                regs = [R_kvreg[p] for p in ps]
                if first:
                    P.emit("vector", lambda e, pb=pb, poff=poff, aoff=aoff, width=width:
                           e.tensor_copy(kvacc[:, aoff:aoff + width], pb[:, poff:poff + width]),
                           reads=regs + [rbank], writes=[R_kvacc], sig="s_v")
                else:
                    P.emit("vector", lambda e, pb=pb, poff=poff, aoff=aoff, width=width:
                           e.tensor_tensor(kvacc[:, aoff:aoff + width], pb[:, poff:poff + width],
                                           kvacc[:, aoff:aoff + width], OP.add),
                           reads=regs + [rbank, R_kvacc], writes=[R_kvacc], sig="s_v")

        def attn_region(p):
            if p < 3:
                return pbA, R_areg[p], 130 * p, R_pA[0]
            if p < 6:
                return pbA, R_areg[p], 512 + 130 * (p - 3), R_pA[1]
            return pbB, R_areg[p], 130 * (p - 6), R_pB[0]

        def emit_attn(i):
            for p in range(NPAIR):
                pb, rp, off, rbank = attn_region(p)
                P.emit("tensor", lambda e, pb=pb, off=off, p=p, i=i: e.matmul(
                    pb[:, off:off + 130], lhsT=qfm[:, i, p, :], rhs=rhsB[:, p, :],
                    start=True, stop=True),
                    reads=[R_qfm[i], R_rhsB], writes=[rp, rbank], sig="s_t")

        def emit_attn_evac(d):
            copies = ((pbA, 0, 0, 390, (0, 1, 2), R_pA[0]),
                      (pbA, 512, 390, 390, (3, 4, 5), R_pA[1]),
                      (pbB, 0, 780, 260, (6, 7), R_pB[0]))
            for pb, poff, aoff, width, ps, rbank in copies:
                regs = [R_areg[p] for p in ps]
                P.emit("scalar", lambda e, pb=pb, poff=poff, aoff=aoff, width=width:
                       e.activation(at_ps[:, aoff:aoff + width], pb[:, poff:poff + width],
                                    AF.Copy, bias=0.0, scale=1.0),
                       reads=regs + [rbank], writes=[R_atps], sig="s_s")
            for p in range(NPAIR):
                P.emit("scalar", lambda e, p=p: e.activation(
                    rnorm[:, 2 * p:2 * p + 2], at_ps[:, 130 * p + 128:130 * p + 130], AF.Abs,
                    bias=epsat[:, :], scale=1.0),
                    reads=[R_atps, R_epsat], writes=[R_rnorm], sig="s_s")
            P.emit("vector", lambda e: e.reciprocal(rnrec[:, :], rnorm[:, :]),
                   reads=[R_rnorm], writes=[R_rnrec], sig="s_v")
            for p in range(NPAIR):
                P.emit("vector", lambda e, p=p: e.tensor_scalar(
                    at_tm[:, p * 128: p * 128 + 64],
                    at_ps[:, p * 130: p * 130 + 64],
                    rnrec[:, 2 * p: 2 * p + 1], None, OP.mult),
                    reads=[R_atps, R_rnrec], writes=[R_attm], sig="s_v")
                P.emit("scalar", lambda e, p=p: e.activation(
                    at_tm[:, p * 128 + 64: p * 128 + 128],
                    at_ps[:, p * 130 + 64: p * 130 + 128], AF.Copy,
                    bias=0.0, scale=rnrec[:, 2 * p + 1: 2 * p + 2]),
                    reads=[R_atps, R_rnrec], writes=[R_attm], sig="s_s")
            P.emit("sync", lambda e, d=d: e.dma_start_transpose(at_fm[d][:, :, :], at_tm[:, :]),
                   reads=[R_attm], writes=[R_atfm[d]], sig="s_tpat", dma=True)

        for s in range(NSEQ):
            kv_first.pop(s, None)
            y1_last_ev.pop(s, None)

            def qkv_block(i):
                d = i % 2
                fm = i % 3
                qkv_part_mms(fm, pbB, R_pB, 0)
                phi_evac(pbB, R_pB, q_tm[:, :], R_qtm)
                P.emit("scalar", lambda e, i=i: e.dma_start_transpose(qfm[:, i, :, :], q_tm[:, :]),
                       reads=[R_qtm], writes=[R_qfm[i]], sig="s_tpq", dma=True)
                qkv_part_mms(fm, pbC, R_pC, D)
                phi_evac(pbC, R_pC, k_tm[:, :], R_ktm)
                qkv_part_mms(fm, pbB, R_pB, 2 * D)
                P.emit("scalar", lambda e, d=d: e.activation(
                    vpad3[d][:, :, 0:64], pbB[:, :], AF.Copy, bias=0.0, scale=1.0),
                    reads=[R_pB[0], R_pB[1]], writes=[R_vpad[d]], sig="s_s")

            load_x(s, 0)
            load_x(s, 1)
            load_x(s, 2)
            load_x(s, 3)
            quant_chain(0, 0, have_mean=False, fm=0)
            quant_chain(1, 1, have_mean=False, fm=1)
            ffn_mms(0, pbA, R_pA)
            xb_pre(0, hbrow, R_hbrow)
            ffn_evac(s, 0, 0, 0, pbA, R_pA, True, True)
            quant_chain(0, 2, have_mean=False, fm=2)
            ln2_chain(0, 0, fm=0)
            for i in range(NCH + 1):
                if i + 4 < NCH:
                    load_x(s, i + 4)
                if i + 1 < NCH:
                    ffn_mms((i + 1) % 3, pbA, R_pA)
                    xb_pre((i + 1) % NXY, hbrow, R_hbrow)
                    ffn_evac(s, i + 1, (i + 1) % 2, (i + 1) % NXY, pbA, R_pA, True, True)
                if i + 3 < NCH:
                    quant_chain((i + 3) % 2, (i + 3) % NXY, have_mean=False, fm=(i + 3) % 3)
                if i >= 1:
                    qkv_block(i - 1)
                    emit_kv(s, i - 1, (i - 1) % 2)
                if i + 1 < NCH:
                    ln2_chain((i + 1) % 2, (i + 1) % NXY, fm=(i + 1) % 3)

            def build_rhsB(e):
                last = None
                for p in range(NPAIR):
                    c0 = 130 * p
                    last = e.tensor_copy(rhsB[0:64, p, 0:64], kvacc[0:64, c0:c0 + 64])
                    last = e.tensor_copy(rhsB[64:128, p, 64:128], kvacc[64:128, c0 + 65:c0 + 129])
                    last = e.tensor_copy(rhsB[0:64, p, 128:129], kvacc[0:64, c0 + 64:c0 + 65])
                    last = e.tensor_copy(rhsB[64:128, p, 129:130], kvacc[64:128, c0 + 64:c0 + 65])
                return last
            P.emit("gpsimd", build_rhsB, reads=[R_kvacc], writes=[R_rhsB], sig="s_g")
            if debug and s == 0:
                P.emit("sync", lambda e: e.dma_start(out=kvd_h, in_=kvacc[:, :]),
                       reads=[R_kvacc], sig="s_dbgk", dma=True)
                P.emit("sync", lambda e: e.dma_start(out=rhsBd_h, in_=rhsB[:, :, :]),
                       reads=[R_rhsB], sig="s_dbgr", dma=True)
                P.emit("sync", lambda e: e.dma_start(out=qfmd_h, in_=qfm[:, :, :, :]),
                       reads=list(R_qfm), sig="s_dbgq", dma=True)

            load_y1(s, 0)
            load_y1(s, 1)
            emit_attn(0)
            emit_attn_evac(0)
            for i in range(NCH + 3):
                if i + 2 < NCH:
                    load_y1(s, i + 2)
                if i + 1 < NCH:
                    emit_attn(i + 1)
                    emit_attn_evac((i + 1) % 3)
                if 1 <= i < NCH + 1:
                    def proj_fn(e, d=(i - 1) % 3):
                        last = None
                        for cc in range(NCC):
                            for ob in range(2):
                                last = e.matmul(pbC[:, ob * 512:(ob + 1) * 512],
                                                lhsT=at_fm[d][:, cc, :],
                                                rhs=Wp[:, cc, ob * 512:(ob + 1) * 512],
                                                start=(cc == 0), stop=(cc == NCC - 1))
                            if cc == 0:
                                for ob in range(2):
                                    e.matmul(pbC[:, ob * 512:(ob + 1) * 512],
                                             lhsT=ones1b[:, :],
                                             rhs=bprow[0:1, ob * 512:(ob + 1) * 512],
                                             start=False, stop=False)
                        return last
                    P.emit("tensor", proj_fn, reads=[R_atfm[(i - 1) % 3], R_onesb, R_bprow] + R_Wp,
                           writes=[R_pC[0], R_pC[1]], sig="s_t")
                if 3 <= i < NCH + 3:
                    ffn_mms((i - 3) % 3, pbD, R_pD)
                if i >= 3:
                    j = i - 3
                    ffn_evac(s, j, j % 2, j % NXY, pbD, R_pD, False, False)
                    bj = j % NXY
                    dst = out_h[s * T + j * CH: s * T + (j + 1) * CH, :]
                    P.emit("gpsimd", lambda e, dst=dst, bj=bj: e.dma_start(out=dst, in_=xy[bj][:, :]),
                           reads=[R_xy[bj]], sig=f"s_out{bj}", dma=True)
                if 1 <= i < NCH + 1:
                    j = i - 1
                    b = j % NXY
                    d = j % 2
                    P.emit("vector", lambda e, b=b, d=d: e.scalar_tensor_tensor(
                        xy[b][:, :], pbC[:, :], 1.0, xy[b][:, :], OP.mult, OP.add, accum_out=msum2[d][:, :]),
                        reads=[R_pC[0], R_pC[1], R_xy[b]], writes=[R_xy[b], RS(msum2, d)], sig="s_v")
                    if debug and s == 0:
                        dst2 = y2d_h[j * CH: (j + 1) * CH, :]
                        P.emit("gpsimd", lambda e, dst2=dst2, b=b: e.dma_start(out=dst2, in_=xy[b][:, :]),
                               reads=[R_xy[b]], sig=f"s_dbg{b % 4}", dma=True)
                    quant_chain(d, b, have_mean=True, fm=j % 3)
                    xb_pre(b, hbrow, R_hbrow)

        fins = {k: P.count[k] for k in P.count
                if k.startswith(("s_out", "s_y1w", "s_dbg")) and P.count[k] > 0}

        @block.sync
        def _(e):
            P.section("sync", e)

        @block.gpsimd
        def _(e):
            P.section("gpsimd", e)
            for k, v in fins.items():
                e.wait_ge(sems[k], v)

        @block.vector
        def _(e):
            P.section("vector", e)

        @block.scalar
        def _(e):
            P.section("scalar", e)

        @block.tensor
        def _(e):
            P.section("tensor", e)

    return nc


_NC_CACHE = None


def kernel(L0_attn_x, L0_attn_Wqkv, L0_attn_Wproj, L0_attn_bproj,
           L0_ffn_weight_q, L0_ffn_weight_scale, L0_ffn_bias):
    global _NC_CACHE
    x = np.ascontiguousarray(np.asarray(L0_attn_x, dtype=np.float32))
    wqkvT = np.ascontiguousarray(np.asarray(L0_attn_Wqkv, np.float32).T)
    wprojT = np.ascontiguousarray(np.asarray(L0_attn_Wproj, np.float32).T)
    wffnT = np.ascontiguousarray(np.asarray(L0_ffn_weight_q).T)
    wscale = np.ascontiguousarray(np.asarray(L0_ffn_weight_scale, np.float32).reshape(1, -1))
    fbias = np.ascontiguousarray(np.asarray(L0_ffn_bias, np.float32).reshape(1, -1))
    bproj = np.ascontiguousarray(np.asarray(L0_attn_bproj, np.float32).reshape(1, -1))

    if _NC_CACHE is None:
        _NC_CACHE = build_nc()
    nc = _NC_CACHE
    in_maps = []
    for c in range(NCORE):
        xs = np.ascontiguousarray(x[2 * c:2 * c + 2].reshape(NSEQ * T, D))
        in_maps.append({"x": xs, "wqkvT": wqkvT, "wprojT": wprojT, "wffnT": wffnT,
                        "wscale": wscale, "fbias": fbias, "bproj": bproj})
    res = run_bass_kernel_spmd(nc, in_maps, core_ids=list(range(NCORE)))
    out = np.empty((16, T, D), np.float32)
    for c in range(NCORE):
        out[2 * c:2 * c + 2] = np.asarray(res.results[c]["out"]).reshape(NSEQ, T, D)
    return out


if __name__ == "__main__":
    import reference as _r
    ins = _r.setup_inputs()
    ins = {k: np.asarray(v) for k, v in ins.items()}
    got = kernel(**ins)
    want = np.asarray(_r.reference(**ins))
    g = got.astype(np.float64)
    w = want.astype(np.float64)
    print("res-var ratio:", float(np.mean((g - w) ** 2) / np.mean(w ** 2)))
    print("max abs:", float(np.max(np.abs(g - w))))


# version 30
# speedup vs baseline: 1.0757x; 1.0757x over previous
import numpy as np
import concourse.bass as bass
import concourse.mybir as mybir
from concourse.bass_utils import run_bass_kernel_spmd
from contextlib import ExitStack

F32 = mybir.dt.float32
BF16 = mybir.dt.bfloat16
I8 = mybir.dt.int8
AX = mybir.AxisListType.X
AF = mybir.ActivationFunctionType
OP = mybir.AluOpType

NCORE = 8
NSEQ = 2
T = 2048
CH = 128
NCH = T // CH
D = 1024
NCC = D // 128
NPAIR = 8
EPS_LN = 1e-5
EPS_ATTN = 1e-6
INV_D = 1.0 / 1024.0
NXY = 6


class Ev:
    __slots__ = ("sem", "val", "eng", "is_async")

    def __init__(self, sem, val, eng, is_async):
        self.sem, self.val, self.eng, self.is_async = sem, val, eng, is_async


class Res:
    __slots__ = ("w", "r")

    def __init__(self):
        self.w = None
        self.r = []


class Prog:
    def __init__(self, sems):
        self.sems = sems
        self.count = {k: 0 for k in sems}
        self.steps = {e: [] for e in ("sync", "gpsimd", "vector", "scalar", "tensor")}

    GP_COMPUTE_TO_VECTOR = False

    def emit(self, eng, fn, reads=(), writes=(), deps=(), sig=None, dma=False):
        if Prog.GP_COMPUTE_TO_VECTOR and eng == "gpsimd" and not dma:
            eng = "vector"
        evs = list(deps)
        for r in reads:
            if r.w is not None:
                evs.append(r.w)
        for w in writes:
            if w.w is not None:
                evs.append(w.w)
            evs.extend(w.r)
        need = {}
        for ev in evs:
            if ev is None:
                continue
            need[ev.sem] = max(need.get(ev.sem, 0), ev.val)
        ev_out = None
        if sig is not None:
            amt = 16 if dma else 1
            self.count[sig] += amt
            ev_out = Ev(sig, self.count[sig], eng, dma)
        self.steps[eng].append((need, fn, sig, 16 if dma else 1))
        for r in reads:
            if ev_out is not None:
                r.r.append(ev_out)
        for w in writes:
            w.w = ev_out
            w.r = []
        return ev_out

    def section(self, eng_name, eng):
        satisfied = {}
        for need, fn, sig, amt in self.steps[eng_name]:
            for k in sorted(need):
                v = need[k]
                if satisfied.get(k, 0) < v:
                    eng.wait_ge(self.sems[k], v)
                    satisfied[k] = v
            inst = fn(eng)
            if sig is not None:
                inst.then_inc(self.sems[sig], amt)


def build_nc(debug=False):
    nc = bass.Bass("TRN2", target_bir_lowering=False)
    x_h = nc.dram_tensor("x", [NSEQ * T, D], F32, kind="ExternalInput").ap()
    wqkvT_h = nc.dram_tensor("wqkvT", [D, 3 * D], F32, kind="ExternalInput").ap()
    wprojT_h = nc.dram_tensor("wprojT", [D, D], F32, kind="ExternalInput").ap()
    wffnT_h = nc.dram_tensor("wffnT", [D, D], I8, kind="ExternalInput").ap()
    wscale_h = nc.dram_tensor("wscale", [1, D], F32, kind="ExternalInput").ap()
    fbias_h = nc.dram_tensor("fbias", [1, D], F32, kind="ExternalInput").ap()
    bproj_h = nc.dram_tensor("bproj", [1, D], F32, kind="ExternalInput").ap()
    out_h = nc.dram_tensor("out", [NSEQ * T, D], F32, kind="ExternalOutput").ap()
    y1sp_h = nc.dram_tensor("y1sp", [NSEQ * T, D], F32,
                            kind="ExternalOutput" if debug else "Internal").ap()
    if debug:
        y2d_h = nc.dram_tensor("y2d", [NSEQ * T, D], F32, kind="ExternalOutput").ap()
        kvd_h = nc.dram_tensor("kvd", [128, NPAIR * 130], F32, kind="ExternalOutput").ap()
        qfmd_h = nc.dram_tensor("qfmd", [128, NCH, NPAIR, 128], BF16, kind="ExternalOutput").ap()
        rhsBd_h = nc.dram_tensor("rhsBd", [128, NPAIR, 130], BF16, kind="ExternalOutput").ap()
        atd_h = nc.dram_tensor("atd", [128, NCH, D], BF16, kind="ExternalOutput").ap()

    with ExitStack() as st:
        def sb(name, shape, dt):
            return st.enter_context(nc.sbuf_tensor(name, shape, dt))

        def psb(name, shape, dt):
            return st.enter_context(nc.psum_tensor(name, shape, dt))

        Wq = sb("Wq", [128, NCC, 3 * D], BF16)
        Wp = sb("Wp", [128, NCC, D], BF16)
        Wf = sb("Wf", [128, NCC, D], BF16)
        Wfi8 = sb("Wfi8", [128, 2, D], I8)
        wsrow = sb("wsrow", [128, D], F32)
        hbrow = sb("hbrow", [128, D], BF16)
        bprow = sb("bprow", [128, D], BF16)
        rowst = sb("rowst", [1, D], F32)
        ones1 = sb("ones1", [1, 128], F32)
        ones1b = sb("ones1b", [1, 128], BF16)
        xy = [sb(f"xy{j}", [128, D], F32) for j in range(NXY)]
        qpre = sb("qpre", [128, D], F32)
        sgn = sb("sgn", [128, D], F32)
        rlu = sb("rlu", [128, D], F32)
        xq_i8 = sb("xq_i8", [128, D], I8)
        xq_tm = sb("xq_tm", [128, D], BF16)
        xq_fm = [sb(f"xq_fm{j}", [128, NCC, 128], BF16) for j in range(2)]
        ln_tm = sb("ln_tm", [128, D], BF16)
        ln_fm = [sb(f"ln_fm{j}", [128, NCC, 128], BF16) for j in range(2)]
        q_tm = sb("q_tm", [128, D], BF16)
        qfm = sb("qfm", [128, NCH, NPAIR, 128], BF16)
        k_tm = sb("k_tm", [128, D], BF16)
        vpad = [sb(f"vpad{j}", [128, 16 * 65], BF16) for j in range(2)]
        kvacc = sb("kvacc", [128, NPAIR * 130], F32)
        rhsB = sb("rhsB", [128, NPAIR, 130], BF16)
        at_tm = sb("at_tm", [128, D], BF16)
        at_fm = [sb(f"at_fm{j}", [128, NCC, 128], BF16) for j in range(3)]
        at_ps = sb("at_ps", [128, NPAIR * 130], F32)
        rnorm = sb("rnorm", [128, 16], F32)
        rnrec = sb("rnrec", [128, 16], F32)

        def s2(name):
            return [sb(name + str(j), [128, 1], F32) for j in range(2)]
        msum = s2("msum")
        mmean = s2("mmean")
        negm = s2("negm")
        vsum = s2("vsum")
        sstd = s2("sstd")
        rstd = s2("rstd")
        mx = s2("mx")
        mn = s2("mn")
        tt1 = s2("tt1")
        tt2 = s2("tt2")
        amax = s2("amax")
        xsc = s2("xsc")
        rxs = s2("rxs")
        gq = s2("gq")
        xs05 = s2("xs05")
        msum2 = s2("msum2")
        epsln = sb("epsln", [128, 1], F32)
        epsat = sb("epsat", [128, 1], F32)
        dt_f = sb("dt_f", [128, 1], F32)
        dt_i8 = sb("dt_i8", [128, 1], I8)
        dt_f2 = sb("dt_f2", [128, 1], F32)
        hdet = sb("hdet", [128, 1], F32)

        pbA = psb("pbA", [128, 1024], F32)
        pbB = psb("pbB", [128, 1024], F32)
        pbC = psb("pbC", [128, 1024], F32)
        pbD = psb("pbD", [128, 1024], F32)

        sems = {}
        for name in ("s_x0", "s_x1", "s_x2", "s_x3", "s_x4", "s_x5", "s_row", "s_wf", "s_wq", "s_wp",
                     "s_tpxq", "s_tpln", "s_tpat", "s_tpq",
                     "s_y1w0", "s_y1w1", "s_y1w2", "s_y1w3", "s_y1w4", "s_y1w5",
                     "s_out0", "s_out1", "s_out2", "s_out3", "s_out4", "s_out5",
                     "s_dbg0", "s_dbg1", "s_dbg2", "s_dbg3", "s_dbg4", "s_dbg5",
                     "s_dbgk", "s_dbgq", "s_dbgr", "s_dbga",
                     "s_t", "s_v", "s_s", "s_g"):
            sems[name] = st.enter_context(nc.semaphore(name))
        block = st.enter_context(nc.Block())

        P = Prog(sems)

        R_xy = [Res() for _ in range(NXY)]
        R_Wq = [Res() for _ in range(NCC)]
        R_Wp = [Res() for _ in range(NCC)]
        R_Wf, R_Wfi8 = Res(), Res()
        R_wsrow, R_hbrow, R_bprow, R_rowst, R_ones = Res(), Res(), Res(), Res(), Res()
        R_onesb = Res()
        R_qpre, R_sgn, R_rlu = Res(), Res(), Res()
        R_xqi8, R_xqtm = Res(), Res()
        R_xqfm = [Res(), Res()]
        R_lntm = Res()
        R_lnfm = [Res(), Res()]
        R_qtm = Res()
        R_qfm = [Res() for _ in range(NCH)]
        R_ktm = Res()
        R_vpad = [Res(), Res()]
        R_kvacc = Res()
        R_rhsB = Res()
        R_attm = Res()
        R_atfm = [Res(), Res(), Res()]
        R_atps = Res()
        R_rnorm, R_rnrec = Res(), Res()
        stat_res = {}
        for t in (msum, mmean, negm, vsum, sstd, rstd, mx, mn, tt1, tt2, amax, xsc, rxs, gq, xs05, msum2):
            stat_res[id(t)] = [Res(), Res()]

        def RS(t, d):
            return stat_res[id(t)][d]
        R_eps, R_epsat, R_det = Res(), Res(), Res()
        R_pA = [Res(), Res()]
        R_pB = [Res(), Res()]
        R_pC = [Res(), Res()]
        R_pD = [Res(), Res()]
        R_areg = [Res() for _ in range(NPAIR)]
        R_kvreg = [Res() for _ in range(NPAIR)]
        R_y1ch = [[Res() for _ in range(NCH)] for _ in range(NSEQ)]

        vpad3 = [vpad[j][:, :].rearrange("p (h m) -> p h m", m=65) for j in range(2)]

        P.emit("vector", lambda e: e.memset(ones1[:, :], 1.0), writes=[R_ones], sig="s_v")
        P.emit("vector", lambda e: e.memset(ones1b[:, :], 1.0), writes=[R_onesb], sig="s_v")
        P.emit("vector", lambda e: e.memset(epsln[:, :], EPS_LN), writes=[R_eps], sig="s_v")
        P.emit("vector", lambda e: e.memset(epsat[:, :], EPS_ATTN), writes=[R_epsat], sig="s_v")
        P.emit("vector", lambda e: e.memset(dt_f[:, :], 0.7), writes=[R_det], sig="s_v")
        P.emit("vector", lambda e: e.tensor_copy(dt_i8[:, :], dt_f[:, :]), reads=[R_det], writes=[R_det], sig="s_v")
        P.emit("vector", lambda e: e.tensor_copy(dt_f2[:, :], dt_i8[:, :]), reads=[R_det], writes=[R_det], sig="s_v")
        P.emit("vector", lambda e: e.tensor_scalar(hdet[:, :], dt_f2[:, :], -1.0, -0.5, OP.add, OP.mult),
               reads=[R_det], writes=[R_det], sig="s_v")
        for j in range(2):
            P.emit("vector", lambda e, j=j: e.memset(vpad3[j][:, :, 64:65], 1.0),
                   writes=[R_vpad[j]], sig="s_v")
        P.emit("vector", lambda e: e.memset(rhsB[:, :, :], 0.0), writes=[R_rhsB], sig="s_v")

        for src, dst_row, scale, rres in ((wscale_h, wsrow, None, R_wsrow),
                                          (fbias_h, hbrow, 0.5, R_hbrow),
                                          (bproj_h, bprow, None, R_bprow)):
            P.emit("sync", lambda e, src=src: e.dma_start(out=rowst[:, :], in_=src),
                   writes=[R_rowst], sig="s_row", dma=True)
            for half in range(2):
                P.emit("tensor", lambda e, half=half: e.matmul(
                    pbA[:, half * 512:(half + 1) * 512], lhsT=ones1[:, :],
                    rhs=rowst[:, half * 512:(half + 1) * 512], start=True, stop=True),
                    reads=[R_rowst, R_ones], writes=[R_pA[half]], sig="s_t")
            if scale is None:
                P.emit("vector", lambda e, dst_row=dst_row: e.tensor_copy(dst_row[:, :], pbA[:, :]),
                       reads=[R_pA[0], R_pA[1]], writes=[rres], sig="s_v")
            else:
                P.emit("vector", lambda e, dst_row=dst_row, scale=scale:
                       e.tensor_scalar(dst_row[:, :], pbA[:, :], scale, None, OP.mult),
                       reads=[R_pA[0], R_pA[1]], writes=[rres], sig="s_v")

        for r in range(4):
            src = wffnT_h[r * 256:(r + 1) * 256, :].rearrange("(cc p) o -> p cc o", p=128)
            P.emit("sync", lambda e, src=src: e.dma_start(out=Wfi8[:, :, :], in_=src),
                   writes=[R_Wfi8], sig="s_wf", dma=True)
            P.emit("vector", lambda e, r=r: e.tensor_copy(Wf[:, 2 * r:2 * r + 2, :], Wfi8[:, :, :]),
                   reads=[R_Wfi8], writes=[R_Wf], sig="s_v")
        for cc in range(NCC):
            src = wqkvT_h[cc * 128:(cc + 1) * 128, :]
            P.emit("gpsimd", lambda e, src=src, cc=cc: e.dma_start(out=Wq[:, cc, :], in_=src),
                   writes=[R_Wq[cc]], sig="s_wq", dma=True)
        for cc in range(NCC):
            src = wprojT_h[cc * 128:(cc + 1) * 128, :]
            P.emit("gpsimd", lambda e, src=src, cc=cc: e.dma_start(out=Wp[:, cc, :], in_=src),
                   writes=[R_Wp[cc]], sig="s_wp", dma=True)

        kv_first = {}
        y1_last_ev = {}

        def load_x(s, i):
            b = i % NXY
            src = x_h[s * T + i * CH: s * T + (i + 1) * CH, :]
            P.emit("sync", lambda e, src=src, b=b: e.dma_start(out=xy[b][:, :], in_=src),
                   writes=[R_xy[b]], sig=f"s_x{b}", dma=True)

        def load_y1(s, i):
            b = i % NXY
            src = y1sp_h[s * T + i * CH: s * T + (i + 1) * CH, :]
            P.emit("sync", lambda e, src=src, b=b: e.dma_start(out=xy[b][:, :], in_=src),
                   reads=[R_y1ch[s][i]], deps=list(y1_last_ev.get(s, {}).values()),
                   writes=[R_xy[b]], sig=f"s_x{b}", dma=True)

        def quant_chain(d, b, have_mean):
            if not have_mean:
                P.emit("vector", lambda e: e.tensor_reduce(msum[d][:, :], xy[b][:, :], AX, OP.add),
                       reads=[R_xy[b]], writes=[RS(msum, d)], sig="s_v")
                src_ms = msum
            else:
                src_ms = msum2
            P.emit("vector", lambda e: e.tensor_reduce(mx[d][:, :], xy[b][:, :], AX, OP.max),
                   reads=[R_xy[b]], writes=[RS(mx, d)], sig="s_v")
            P.emit("vector", lambda e: e.tensor_reduce(mn[d][:, :], xy[b][:, :], AX, OP.min),
                   reads=[R_xy[b]], writes=[RS(mn, d)], sig="s_v")
            P.emit("gpsimd", lambda e: e.tensor_scalar(mmean[d][:, :], src_ms[d][:, :], INV_D, None, OP.mult),
                   reads=[RS(src_ms, d)], writes=[RS(mmean, d)], sig="s_g")
            P.emit("gpsimd", lambda e: e.tensor_scalar(negm[d][:, :], src_ms[d][:, :], -INV_D, None, OP.mult),
                   reads=[RS(src_ms, d)], writes=[RS(negm, d)], sig="s_g")
            P.emit("scalar", lambda e: e.activation(qpre[:, :], xy[b][:, :], AF.Square,
                                                    bias=negm[d][:, :], accum_out=vsum[d][:, :]),
                   reads=[R_xy[b], RS(negm, d)], writes=[RS(vsum, d), R_qpre], sig="s_s")
            P.emit("scalar", lambda e: e.activation(sgn[:, :], xy[b][:, :], AF.Sign,
                                                    bias=negm[d][:, :]),
                   reads=[R_xy[b], RS(negm, d)], writes=[R_sgn], sig="s_s")
            P.emit("scalar", lambda e: e.activation(sstd[d][:, :], vsum[d][:, :], AF.Sqrt,
                                                    bias=epsln[:, :], scale=INV_D),
                   reads=[RS(vsum, d), R_eps], writes=[RS(sstd, d)], sig="s_s")
            P.emit("vector", lambda e: e.reciprocal(rstd[d][:, :], sstd[d][:, :]),
                   reads=[RS(sstd, d)], writes=[RS(rstd, d)], sig="s_v")
            P.emit("gpsimd", lambda e: e.tensor_scalar(tt1[d][:, :], mx[d][:, :], mmean[d][:, :], None, OP.subtract),
                   reads=[RS(mx, d), RS(mmean, d)], writes=[RS(tt1, d)], sig="s_g")
            P.emit("gpsimd", lambda e: e.tensor_scalar(tt2[d][:, :], mn[d][:, :], mmean[d][:, :], -1.0,
                                                       OP.subtract, OP.mult),
                   reads=[RS(mn, d), RS(mmean, d)], writes=[RS(tt2, d)], sig="s_g")
            P.emit("vector", lambda e: e.tensor_tensor(tt1[d][:, :], tt1[d][:, :], tt2[d][:, :], OP.max),
                   reads=[RS(tt1, d), RS(tt2, d)], writes=[RS(tt1, d)], sig="s_v")
            P.emit("gpsimd", lambda e: e.tensor_scalar(amax[d][:, :], tt1[d][:, :], rstd[d][:, :], None, OP.mult),
                   reads=[RS(tt1, d), RS(rstd, d)], writes=[RS(amax, d)], sig="s_g")
            P.emit("vector", lambda e: e.tensor_scalar(xsc[d][:, :], amax[d][:, :], EPS_ATTN, 1.0 / 127.0,
                                                       OP.max, OP.mult),
                   reads=[RS(amax, d)], writes=[RS(xsc, d)], sig="s_v")
            P.emit("vector", lambda e: e.reciprocal(rxs[d][:, :], xsc[d][:, :]),
                   reads=[RS(xsc, d)], writes=[RS(rxs, d)], sig="s_v")
            P.emit("gpsimd", lambda e: e.tensor_scalar(gq[d][:, :], rstd[d][:, :], rxs[d][:, :], None, OP.mult),
                   reads=[RS(rstd, d), RS(rxs, d)], writes=[RS(gq, d)], sig="s_g")
            P.emit("gpsimd", lambda e: e.tensor_scalar(xs05[d][:, :], xsc[d][:, :], 0.5, None, OP.mult),
                   reads=[RS(xsc, d)], writes=[RS(xs05, d)], sig="s_g")
            P.emit("vector", lambda e: e.tensor_scalar(qpre[:, :], xy[b][:, :], mmean[d][:, :], gq[d][:, :],
                                                       OP.subtract, OP.mult),
                   reads=[R_xy[b], RS(mmean, d), RS(gq, d)], writes=[R_qpre], sig="s_v")
            P.emit("vector", lambda e: e.scalar_tensor_tensor(xq_i8[:, :], sgn[:, :], hdet[:, :], qpre[:, :],
                                                              OP.mult, OP.add),
                   reads=[R_sgn, R_qpre, R_det], writes=[R_xqi8], sig="s_v")
            P.emit("vector", lambda e: e.tensor_copy(xq_tm[:, :], xq_i8[:, :]),
                   reads=[R_xqi8], writes=[R_xqtm], sig="s_v")
            P.emit("sync", lambda e, d=d: e.dma_start_transpose(xq_fm[d][:, :, :], xq_tm[:, :]),
                   reads=[R_xqtm], writes=[R_xqfm[d]], sig="s_tpxq", dma=True)

        def ffn_mms(d, pb, R_p):
            def fn(e):
                last = None
                for cc in range(NCC):
                    for ob in range(2):
                        last = e.matmul(pb[:, ob * 512:(ob + 1) * 512],
                                        lhsT=xq_fm[d][:, cc, :],
                                        rhs=Wf[:, cc, ob * 512:(ob + 1) * 512],
                                        start=(cc == 0), stop=(cc == NCC - 1))
                return last
            P.emit("tensor", fn, reads=[R_xqfm[d], R_Wf], writes=[R_p[0], R_p[1]], sig="s_t")

        def xb_pre(b, row, rrow):
            P.emit("gpsimd", lambda e, b=b, row=row: e.tensor_tensor(xy[b][:, :], xy[b][:, :], row[:, :], OP.add),
                   reads=[R_xy[b], rrow], writes=[R_xy[b]], sig="s_g")

        def ffn_evac(s, i, d, b, pb, R_p, accum_next_mean, store_y1):
            P.emit("vector", lambda e: e.scalar_tensor_tensor(qpre[:, :], pb[:, :], xs05[d][:, :],
                                                              wsrow[:, :], OP.mult, OP.mult),
                   reads=[R_p[0], R_p[1], RS(xs05, d), R_wsrow], writes=[R_qpre], sig="s_v")
            if accum_next_mean:
                P.emit("vector", lambda e: e.scalar_tensor_tensor(xy[b][:, :], qpre[:, :], 1.0, xy[b][:, :],
                                                                  OP.mult, OP.add, accum_out=msum2[d][:, :]),
                       reads=[R_qpre, R_xy[b]], writes=[R_xy[b], RS(msum2, d)], sig="s_v")
            else:
                P.emit("vector", lambda e: e.scalar_tensor_tensor(xy[b][:, :], qpre[:, :], 1.0, xy[b][:, :],
                                                                  OP.mult, OP.add),
                       reads=[R_qpre, R_xy[b]], writes=[R_xy[b]], sig="s_v")
            if store_y1:
                dst = y1sp_h[s * T + i * CH: s * T + (i + 1) * CH, :]
                ev = P.emit("gpsimd", lambda e, dst=dst, b=b: e.dma_start(out=dst, in_=xy[b][:, :]),
                            reads=[R_xy[b]], writes=[R_y1ch[s][i]], sig=f"s_y1w{b}", dma=True)
                y1_last_ev.setdefault(s, {})[b] = ev

        def ln2_chain(d, b):
            P.emit("gpsimd", lambda e: e.tensor_scalar(mmean[d][:, :], msum2[d][:, :], INV_D, None, OP.mult),
                   reads=[RS(msum2, d)], writes=[RS(mmean, d)], sig="s_g")
            P.emit("gpsimd", lambda e: e.tensor_scalar(negm[d][:, :], msum2[d][:, :], -INV_D, None, OP.mult),
                   reads=[RS(msum2, d)], writes=[RS(negm, d)], sig="s_g")
            P.emit("scalar", lambda e: e.activation(qpre[:, :], xy[b][:, :], AF.Square,
                                                    bias=negm[d][:, :], accum_out=vsum[d][:, :]),
                   reads=[R_xy[b], RS(negm, d)], writes=[RS(vsum, d), R_qpre], sig="s_s")
            P.emit("scalar", lambda e: e.activation(sstd[d][:, :], vsum[d][:, :], AF.Sqrt,
                                                    bias=epsln[:, :], scale=INV_D),
                   reads=[RS(vsum, d), R_eps], writes=[RS(sstd, d)], sig="s_s")
            P.emit("vector", lambda e: e.reciprocal(rstd[d][:, :], sstd[d][:, :]),
                   reads=[RS(sstd, d)], writes=[RS(rstd, d)], sig="s_v")
            P.emit("vector", lambda e: e.tensor_scalar(ln_tm[:, :], xy[b][:, :], mmean[d][:, :], rstd[d][:, :],
                                                       OP.subtract, OP.mult),
                   reads=[R_xy[b], RS(mmean, d), RS(rstd, d)], writes=[R_lntm], sig="s_v")
            P.emit("scalar", lambda e, d=d: e.dma_start_transpose(ln_fm[d][:, :, :], ln_tm[:, :]),
                   reads=[R_lntm], writes=[R_lnfm[d]], sig="s_tpln", dma=True)

        def qkv_part_mms(d, pb, R_p, e_lo):
            def fn(e):
                last = None
                for cc in range(NCC):
                    for ob in range(2):
                        last = e.matmul(pb[:, ob * 512:(ob + 1) * 512],
                                        lhsT=ln_fm[d][:, cc, :],
                                        rhs=Wq[:, cc, e_lo + ob * 512: e_lo + (ob + 1) * 512],
                                        start=(cc == 0), stop=(cc == NCC - 1))
                return last
            P.emit("tensor", fn, reads=[R_lnfm[d]] + R_Wq, writes=[R_p[0], R_p[1]], sig="s_t")

        def phi_evac(pb, R_p, dst_ap, R_dst):
            P.emit("scalar", lambda e: e.activation(sgn[:, :], pb[:, :], AF.Relu, scale=-1.0),
                   reads=[R_p[0], R_p[1]], writes=[R_sgn], sig="s_s")
            P.emit("scalar", lambda e: e.activation(rlu[:, :], sgn[:, :], AF.Exp, scale=-1.0),
                   reads=[R_sgn], writes=[R_rlu], sig="s_s")
            P.emit("vector", lambda e: e.scalar_tensor_tensor(dst_ap, pb[:, :], 0.0, rlu[:, :],
                                                              OP.max, OP.add),
                   reads=[R_p[0], R_p[1], R_rlu], writes=[R_dst], sig="s_v")

        def kv_region(p):
            if p < 3:
                return pbD, 130 * p, R_pD[0]
            if p < 6:
                return pbD, 512 + 130 * (p - 3), R_pD[1]
            return pbC, 130 * (p - 6), R_pC[0]

        def emit_kv(s, i, d):
            first = kv_first.setdefault(s, i) == i
            for p in range(NPAIR):
                pb, off, rbank = kv_region(p)
                P.emit("tensor", lambda e, pb=pb, off=off, p=p, d=d: e.matmul(
                    pb[:, off:off + 130], lhsT=k_tm[:, p * 128:(p + 1) * 128],
                    rhs=vpad[d][:, p * 130:(p + 1) * 130], start=True, stop=True),
                    reads=[R_ktm, R_vpad[d]], writes=[R_kvreg[p], rbank], sig="s_t")
            groups = ((pbD, 0, 0, 390, (0, 1, 2), R_pD[0]),
                      (pbD, 512, 390, 390, (3, 4, 5), R_pD[1]),
                      (pbC, 0, 780, 260, (6, 7), R_pC[0]))
            for pb, poff, aoff, width, ps, rbank in groups:
                regs = [R_kvreg[p] for p in ps]
                if first:
                    P.emit("vector", lambda e, pb=pb, poff=poff, aoff=aoff, width=width:
                           e.tensor_copy(kvacc[:, aoff:aoff + width], pb[:, poff:poff + width]),
                           reads=regs + [rbank], writes=[R_kvacc], sig="s_v")
                else:
                    P.emit("vector", lambda e, pb=pb, poff=poff, aoff=aoff, width=width:
                           e.tensor_tensor(kvacc[:, aoff:aoff + width], pb[:, poff:poff + width],
                                           kvacc[:, aoff:aoff + width], OP.add),
                           reads=regs + [rbank, R_kvacc], writes=[R_kvacc], sig="s_v")

        def attn_region(p):
            if p < 3:
                return pbA, R_areg[p], 130 * p, R_pA[0]
            if p < 6:
                return pbA, R_areg[p], 512 + 130 * (p - 3), R_pA[1]
            return pbB, R_areg[p], 130 * (p - 6), R_pB[0]

        def emit_attn(i):
            for p in range(NPAIR):
                pb, rp, off, rbank = attn_region(p)
                P.emit("tensor", lambda e, pb=pb, off=off, p=p, i=i: e.matmul(
                    pb[:, off:off + 130], lhsT=qfm[:, i, p, :], rhs=rhsB[:, p, :],
                    start=True, stop=True),
                    reads=[R_qfm[i], R_rhsB], writes=[rp, rbank], sig="s_t")

        def emit_attn_evac(d):
            copies = ((pbA, 0, 0, 390, (0, 1, 2), R_pA[0]),
                      (pbA, 512, 390, 390, (3, 4, 5), R_pA[1]),
                      (pbB, 0, 780, 260, (6, 7), R_pB[0]))
            for pb, poff, aoff, width, ps, rbank in copies:
                regs = [R_areg[p] for p in ps]
                P.emit("scalar", lambda e, pb=pb, poff=poff, aoff=aoff, width=width:
                       e.activation(at_ps[:, aoff:aoff + width], pb[:, poff:poff + width],
                                    AF.Copy, bias=0.0, scale=1.0),
                       reads=regs + [rbank], writes=[R_atps], sig="s_s")
            for p in range(NPAIR):
                P.emit("scalar", lambda e, p=p: e.activation(
                    rnorm[:, 2 * p:2 * p + 2], at_ps[:, 130 * p + 128:130 * p + 130], AF.Abs,
                    bias=epsat[:, :], scale=1.0),
                    reads=[R_atps, R_epsat], writes=[R_rnorm], sig="s_s")
            P.emit("vector", lambda e: e.reciprocal(rnrec[:, :], rnorm[:, :]),
                   reads=[R_rnorm], writes=[R_rnrec], sig="s_v")
            for p in range(NPAIR):
                P.emit("vector", lambda e, p=p: e.tensor_scalar(
                    at_tm[:, p * 128: p * 128 + 64],
                    at_ps[:, p * 130: p * 130 + 64],
                    rnrec[:, 2 * p: 2 * p + 1], None, OP.mult),
                    reads=[R_atps, R_rnrec], writes=[R_attm], sig="s_v")
                P.emit("scalar", lambda e, p=p: e.activation(
                    at_tm[:, p * 128 + 64: p * 128 + 128],
                    at_ps[:, p * 130 + 64: p * 130 + 128], AF.Copy,
                    bias=0.0, scale=rnrec[:, 2 * p + 1: 2 * p + 2]),
                    reads=[R_atps, R_rnrec], writes=[R_attm], sig="s_s")
            P.emit("sync", lambda e, d=d: e.dma_start_transpose(at_fm[d][:, :, :], at_tm[:, :]),
                   reads=[R_attm], writes=[R_atfm[d]], sig="s_tpat", dma=True)

        for s in range(NSEQ):
            kv_first.pop(s, None)
            y1_last_ev.pop(s, None)

            def qkv_block(i):
                d = i % 2
                qkv_part_mms(d, pbB, R_pB, 0)
                phi_evac(pbB, R_pB, q_tm[:, :], R_qtm)
                P.emit("scalar", lambda e, i=i: e.dma_start_transpose(qfm[:, i, :, :], q_tm[:, :]),
                       reads=[R_qtm], writes=[R_qfm[i]], sig="s_tpq", dma=True)
                qkv_part_mms(d, pbC, R_pC, D)
                phi_evac(pbC, R_pC, k_tm[:, :], R_ktm)
                qkv_part_mms(d, pbB, R_pB, 2 * D)
                P.emit("scalar", lambda e, d=d: e.activation(
                    vpad3[d][:, :, 0:64], pbB[:, :], AF.Copy, bias=0.0, scale=1.0),
                    reads=[R_pB[0], R_pB[1]], writes=[R_vpad[d]], sig="s_s")

            load_x(s, 0)
            load_x(s, 1)
            quant_chain(0, 0, have_mean=False)
            ffn_mms(0, pbA, R_pA)
            quant_chain(1, 1, have_mean=False)
            xb_pre(0, hbrow, R_hbrow)
            ffn_evac(s, 0, 0, 0, pbA, R_pA, True, True)
            ln2_chain(0, 0)
            for i in range(NCH):
                if i + 2 < NCH:
                    load_x(s, i + 2)
                if i + 1 < NCH:
                    ffn_mms((i + 1) % 2, pbA, R_pA)
                if i + 2 < NCH:
                    quant_chain((i + 2) % 2, (i + 2) % NXY, have_mean=False)
                if i + 1 < NCH:
                    xb_pre((i + 1) % NXY, hbrow, R_hbrow)
                    ffn_evac(s, i + 1, (i + 1) % 2, (i + 1) % NXY, pbA, R_pA, True, True)
                qkv_block(i)
                emit_kv(s, i, i % 2)
                if i + 1 < NCH:
                    ln2_chain((i + 1) % 2, (i + 1) % NXY)

            def build_rhsB(e):
                last = None
                for p in range(NPAIR):
                    c0 = 130 * p
                    last = e.tensor_copy(rhsB[0:64, p, 0:64], kvacc[0:64, c0:c0 + 64])
                    last = e.tensor_copy(rhsB[64:128, p, 64:128], kvacc[64:128, c0 + 65:c0 + 129])
                    last = e.tensor_copy(rhsB[0:64, p, 128:129], kvacc[0:64, c0 + 64:c0 + 65])
                    last = e.tensor_copy(rhsB[64:128, p, 129:130], kvacc[64:128, c0 + 64:c0 + 65])
                return last
            P.emit("gpsimd", build_rhsB, reads=[R_kvacc], writes=[R_rhsB], sig="s_g")
            if debug and s == 0:
                P.emit("sync", lambda e: e.dma_start(out=kvd_h, in_=kvacc[:, :]),
                       reads=[R_kvacc], sig="s_dbgk", dma=True)
                P.emit("sync", lambda e: e.dma_start(out=rhsBd_h, in_=rhsB[:, :, :]),
                       reads=[R_rhsB], sig="s_dbgr", dma=True)
                P.emit("sync", lambda e: e.dma_start(out=qfmd_h, in_=qfm[:, :, :, :]),
                       reads=list(R_qfm), sig="s_dbgq", dma=True)

            load_y1(s, 0)
            load_y1(s, 1)
            emit_attn(0)
            emit_attn_evac(0)
            for i in range(NCH + 3):
                if i + 2 < NCH:
                    load_y1(s, i + 2)
                if i + 1 < NCH:
                    emit_attn(i + 1)
                    emit_attn_evac((i + 1) % 3)
                if 1 <= i < NCH + 1:
                    def proj_fn(e, d=(i - 1) % 3):
                        last = None
                        for cc in range(NCC):
                            for ob in range(2):
                                last = e.matmul(pbC[:, ob * 512:(ob + 1) * 512],
                                                lhsT=at_fm[d][:, cc, :],
                                                rhs=Wp[:, cc, ob * 512:(ob + 1) * 512],
                                                start=(cc == 0), stop=(cc == NCC - 1))
                            if cc == 0:
                                for ob in range(2):
                                    e.matmul(pbC[:, ob * 512:(ob + 1) * 512],
                                             lhsT=ones1b[:, :],
                                             rhs=bprow[0:1, ob * 512:(ob + 1) * 512],
                                             start=False, stop=False)
                        return last
                    P.emit("tensor", proj_fn, reads=[R_atfm[(i - 1) % 3], R_onesb, R_bprow] + R_Wp,
                           writes=[R_pC[0], R_pC[1]], sig="s_t")
                if 3 <= i < NCH + 3:
                    ffn_mms((i - 3) % 2, pbD, R_pD)
                if i >= 3:
                    j = i - 3
                    ffn_evac(s, j, j % 2, j % NXY, pbD, R_pD, False, False)
                    bj = j % NXY
                    dst = out_h[s * T + j * CH: s * T + (j + 1) * CH, :]
                    P.emit("gpsimd", lambda e, dst=dst, bj=bj: e.dma_start(out=dst, in_=xy[bj][:, :]),
                           reads=[R_xy[bj]], sig=f"s_out{bj}", dma=True)
                if 1 <= i < NCH + 1:
                    j = i - 1
                    b = j % NXY
                    d = j % 2
                    P.emit("vector", lambda e, b=b, d=d: e.scalar_tensor_tensor(
                        xy[b][:, :], pbC[:, :], 1.0, xy[b][:, :], OP.mult, OP.add, accum_out=msum2[d][:, :]),
                        reads=[R_pC[0], R_pC[1], R_xy[b]], writes=[R_xy[b], RS(msum2, d)], sig="s_v")
                    if debug and s == 0:
                        dst2 = y2d_h[j * CH: (j + 1) * CH, :]
                        P.emit("gpsimd", lambda e, dst2=dst2, b=b: e.dma_start(out=dst2, in_=xy[b][:, :]),
                               reads=[R_xy[b]], sig=f"s_dbg{b % 4}", dma=True)
                    quant_chain(d, b, have_mean=True)
                    xb_pre(b, hbrow, R_hbrow)

        fins = {k: P.count[k] for k in P.count
                if k.startswith(("s_out", "s_y1w", "s_dbg")) and P.count[k] > 0}

        @block.sync
        def _(e):
            P.section("sync", e)

        @block.gpsimd
        def _(e):
            P.section("gpsimd", e)
            for k, v in fins.items():
                e.wait_ge(sems[k], v)

        @block.vector
        def _(e):
            P.section("vector", e)

        @block.scalar
        def _(e):
            P.section("scalar", e)

        @block.tensor
        def _(e):
            P.section("tensor", e)

    return nc


_NC_CACHE = None


def kernel(L0_attn_x, L0_attn_Wqkv, L0_attn_Wproj, L0_attn_bproj,
           L0_ffn_weight_q, L0_ffn_weight_scale, L0_ffn_bias):
    global _NC_CACHE
    x = np.ascontiguousarray(np.asarray(L0_attn_x, dtype=np.float32))
    wqkvT = np.ascontiguousarray(np.asarray(L0_attn_Wqkv, np.float32).T)
    wprojT = np.ascontiguousarray(np.asarray(L0_attn_Wproj, np.float32).T)
    wffnT = np.ascontiguousarray(np.asarray(L0_ffn_weight_q).T)
    wscale = np.ascontiguousarray(np.asarray(L0_ffn_weight_scale, np.float32).reshape(1, -1))
    fbias = np.ascontiguousarray(np.asarray(L0_ffn_bias, np.float32).reshape(1, -1))
    bproj = np.ascontiguousarray(np.asarray(L0_attn_bproj, np.float32).reshape(1, -1))

    if _NC_CACHE is None:
        _NC_CACHE = build_nc()
    nc = _NC_CACHE
    in_maps = []
    for c in range(NCORE):
        xs = np.ascontiguousarray(x[2 * c:2 * c + 2].reshape(NSEQ * T, D))
        in_maps.append({"x": xs, "wqkvT": wqkvT, "wprojT": wprojT, "wffnT": wffnT,
                        "wscale": wscale, "fbias": fbias, "bproj": bproj})
    res = run_bass_kernel_spmd(nc, in_maps, core_ids=list(range(NCORE)))
    out = np.empty((16, T, D), np.float32)
    for c in range(NCORE):
        out[2 * c:2 * c + 2] = np.asarray(res.results[c]["out"]).reshape(NSEQ, T, D)
    return out


if __name__ == "__main__":
    import reference as _r
    ins = _r.setup_inputs()
    ins = {k: np.asarray(v) for k, v in ins.items()}
    got = kernel(**ins)
    want = np.asarray(_r.reference(**ins))
    g = got.astype(np.float64)
    w = want.astype(np.float64)
    print("res-var ratio:", float(np.mean((g - w) ** 2) / np.mean(w ** 2)))
    print("max abs:", float(np.max(np.abs(g - w))))
